# version 05 (final)
# baseline (speedup 1.0000x reference)
import numpy as np
import concourse.bass as bass
import concourse.mybir as mybir
from concourse.bass_utils import run_bass_kernel_spmd
from concourse.alu_op_type import AluOpType as OP

F32 = mybir.dt.float32
AF = mybir.ActivationFunctionType
AX = mybir.AxisListType

D = 1024
NTOK = 2048
NTT = 16
NCH = 8
NPAIR = 8
SCALE = 0.125
CB_A = -8.0
CB_B = -16.0
PAD_B = -99.0
LN_EPS = 1e-5
RMS_EPS = 1e-6
PARANOID = 0


class Sem:
    def __init__(self, handle, name):
        self.h = handle
        self.name = name
        self.n = 0


class EQ:

    def __init__(self, name, sem):
        self.name = name
        self.q = []
        self.sem = sem
        self.wm = {}

    def wait(self, sem, val):
        if sem is self.sem or val is None or val <= 0:
            return
        if self.wm.get(sem.name, -1) >= val:
            return
        self.wm[sem.name] = val
        self.q.append(lambda eng, s=sem, v=val: eng.wait_ge(s.h, v))

    def emit(self, fn, inc=None, inc_k=1):
        if inc is not None:
            inc.n += inc_k
            val = inc.n
            self.q.append(lambda eng, f=fn, s=inc, k=inc_k: f(eng).then_inc(s.h, k))
            return val
        self.q.append(lambda eng, f=fn: f(eng))
        return None


class Bank:
    def __init__(self, ap, name):
        self.ap = ap
        self.name = name
        self.w = 0
        self.r = {}

    def pre_write(self, teq):
        for _, (eq, v) in self.r.items():
            teq.wait(eq.sem, v)
        self.r = {}

    def set_w(self, val):
        self.w = val

    def pre_read(self, eq, mm_sem):
        eq.wait(mm_sem, self.w)

    def set_r(self, eq, val):
        old = self.r.get(eq.name)
        if old is None or old[1] < val:
            self.r[eq.name] = (eq, val)


class Pool:
    def __init__(self, ap, cols):
        self.ap = ap
        self.cols = cols
        self.off = 0
        self.marks = []

    def alloc(self, n):
        assert self.off + n <= self.cols, f"pool overflow: {self.off}+{n} > {self.cols}"
        a = self.ap[:, self.off:self.off + n]
        self.off += n
        return a

    def push(self):
        self.marks.append(self.off)

    def pop(self):
        self.off = self.marks.pop()


def build_program(dbg=()):
    dbg = list(dbg)
    nc = bass.Bass("TRN2", target_bir_lowering=False)

    ei = lambda n, s: nc.dram_tensor(n, s, F32, kind="ExternalInput").ap()
    eo = lambda n, s: nc.dram_tensor(n, s, F32, kind="ExternalOutput").ap()
    x_in = ei("x", [NTT, 128, D])
    y_out = eo("y", [NTT, 128, D])
    wqa = ei("wqa", [64, 128, 128])
    wdn = ei("wdn", [32, 128, 128])
    wku = ei("wku", [32, 128, 128])
    wvu = ei("wvu", [4, 128, 1024])
    woa = ei("woa", [8, 128, 1024])
    wqb = ei("wqb", [64, 128, 128])
    wkb = ei("wkb", [64, 128, 128])
    wvb = ei("wvb", [8, 128, 1024])
    wob = ei("wob", [8, 128, 1024])
    bqb = ei("bqb", [128, 8])
    bkb = ei("bkb", [128, 8])
    bvb = ei("bvb", [1, 1024])
    bob = ei("bob", [1, 1024])
    nwq = ei("nwq", [128, 8])
    nwk = ei("nwk", [128, 8])
    wg = ei("wg", [10, 64, 128, 128])
    wu = ei("wu", [10, 64, 128, 128])
    wd = ei("wd", [10, 8, 128, 1024])
    wr = ei("wr", [8, 128, 8])
    ropeC = ei("ropeC", [128, NTOK])
    ropeS = ei("ropeS", [128, NTOK])
    identw = ei("identw", [128, 128])
    pbias = ei("pbias", [128, 1])

    di = lambda n, s: nc.dram_tensor(n, s, F32).ap()
    fl_in = di("fl_in", [1, 16])
    fl_out = di("fl_out", [2, 16])
    fl_ar = di("fl_ar", [1, 16])
    q_sp = di("q_sp", [8, 128, NTOK])
    k_in = di("k_in", [1025, NTOK])
    k_out = di("k_out", [2, 1025, NTOK])
    v_in = di("v_in", [NTT, 128, 8, 130])
    v_out = di("v_out", [2, NTT, 128, 8, 130])

    ctxs = []

    def enter(cm):
        ctxs.append(cm)
        return cm.__enter__()

    s_t = Sem(enter(nc.semaphore("s_t")), "s_t")
    s_v = Sem(enter(nc.semaphore("s_v")), "s_v")
    s_s = Sem(enter(nc.semaphore("s_s")), "s_s")
    s_y = Sem(enter(nc.semaphore("s_y")), "s_y")
    s_g = Sem(enter(nc.semaphore("s_g")), "s_g")
    s_c = Sem(enter(nc.semaphore("s_c")), "s_c")

    T = EQ("T", s_t)
    V = EQ("V", s_v)
    S = EQ("S", s_s)
    Y = EQ("Y", s_y)
    G = EQ("G", s_g)
    ALL = (T, V, S, Y, G)

    sb = lambda n, s: enter(nc.sbuf_tensor(n, s, F32)).ap()
    x_tm = sb("x_tm", [128, NTT * 1024])
    ident = sb("ident", [128, 128])
    ones_col = sb("ones_col", [128, 1])
    ones_row = sb("ones_row", [1, 128])
    cba = sb("cba", [128, 1])
    cbb = sb("cbb", [128, 1])
    pbias_sb = sb("pbias_sb", [128, 1])
    bq_sb = sb("bq_sb", [128, 8])
    bk_sb = sb("bk_sb", [128, 8])
    nwq_sb = sb("nwq_sb", [128, 8])
    nwk_sb = sb("nwk_sb", [128, 8])
    rw_sb = sb("rw_sb", [128, 64])
    c_tm = sb("c_tm", [128, 128])
    sml = sb("sml", [128, 64])

    POOLC = 36200
    pool_t = sb("pool", [128, POOLC])
    pool = Pool(pool_t, POOLC)

    P = [enter(nc.psum_tensor(f"p{i}", [128, 512], F32)).ap() for i in range(8)]
    BK = [Bank(P[i], f"p{i}") for i in range(8)]

    block = enter(nc.Block("mk"))

    def flush():
        for eq, attr in ((Y, "sync"), (G, "gpsimd"), (T, "tensor"),
                         (V, "vector"), (S, "scalar")):
            if not eq.q:
                continue
            q = eq.q
            eq.q = []

            def body(eng, q=q):
                for f in q:
                    f(eng)
            getattr(block, attr)(body)

    def barrier_all():
        snap = [(e.sem, e.sem.n) for e in ALL]
        for e in ALL:
            for sm, v in snap:
                e.wait(sm, v)

    dbg_requests = []

    def dbg_dump(tag, src_ap, shape):
        if tag not in dbg:
            return
        o = eo("dbg_" + tag, list(shape))
        vhand()
        shand()
        barrier_all()
        G.emit(lambda g, o=o, s=src_ap: g.dma_start(out=o[:], in_=s), inc=s_g, inc_k=16)
        dbg_requests.append(tag)

    for dst, src in ((ident, identw), (pbias_sb, pbias), (bq_sb, bqb),
                     (bk_sb, bkb), (nwq_sb, nwq), (nwk_sb, nwk)):
        G.emit(lambda g, d=dst, s=src: g.dma_start(out=d[:], in_=s[:]), inc=s_g, inc_k=16)
    G.emit(lambda g: g.dma_start(out=rw_sb.rearrange("p (c e) -> p c e", e=8),
                                 in_=wr.rearrange("c p e -> p c e")),
           inc=s_g, inc_k=16)
    x_v = x_tm.rearrange("p (t c) -> p t c", c=1024)
    G.emit(lambda g: g.dma_start(
        out=x_v, in_=x_in.rearrange("t p c -> p t c")), inc=s_g, inc_k=16)
    g_x = G.emit(lambda g: g.drain(), inc=s_g, inc_k=16)
    g_consts = g_x
    def vsep():
        V.emit(lambda v: v.tensor_copy(sml[:, 7:8], sml[:, 6:7]))

    def vhand():
        return V.emit(lambda v: v.tensor_copy(sml[:, 7:8], sml[:, 6:7]), inc=s_v)

    def shand():
        return S.emit(lambda s: s.activation(sml[:, 14:15], sml[:, 14:15],
                                             AF.Copy), inc=s_s)

    V.wait(s_g, g_consts)
    V.emit(lambda v: v.memset(ones_col[:], 1.0))
    V.emit(lambda v: v.memset(ones_row[:], 1.0))
    V.emit(lambda v: v.memset(cba[:], CB_A))
    V.emit(lambda v: v.memset(cbb[:], CB_B))
    v_consts = vhand()

    def emit_ln(xn_fm, xn_scr, tile_lo=0, ntiles=NTT):
        cs = ntiles * 128
        s1 = sml[:, 0:1]; s2 = sml[:, 1:2]; mme = sml[:, 2:3]
        var = sml[:, 3:4]; rstd = sml[:, 4:5]; mrs = sml[:, 5:6]
        V.wait(s_g, g_x)
        tv_last = 0
        for ii in range(ntiles):
            i = tile_lo + ii
            xt = x_tm[:, i * 1024:(i + 1) * 1024]
            V.wait(s_t, tv_last)
            V.emit(lambda v, xt=xt: v.tensor_tensor(xn_scr[:], xt, xt,
                                                     op=OP.mult))
            V.emit(lambda v, xt=xt: v.reduce_sum(s1, xt, axis=AX.X))
            V.emit(lambda v: v.reduce_sum(s2, xn_scr[:], axis=AX.X))
            V.emit(lambda v: v.tensor_scalar(mme, s1, 1.0 / D, None, op0=OP.mult))
            V.emit(lambda v: v.tensor_scalar(s2, s2, 1.0 / D, LN_EPS,
                                             op0=OP.mult, op1=OP.add))
            V.emit(lambda v: v.tensor_tensor(var, mme, mme, op=OP.mult))
            vsep()
            V.emit(lambda v: v.tensor_tensor(var, s2, var, op=OP.subtract))
            vsep()
            V.emit(lambda v: v.reciprocal(rstd, var))
            vv = vhand()
            S.wait(s_v, vv)
            S.emit(lambda s: s.activation(rstd, rstd, AF.Sqrt))
            sv = shand()
            V.wait(s_s, sv)
            V.emit(lambda v: v.tensor_scalar(mrs, mme, rstd, -1.0,
                                             op0=OP.mult, op1=OP.mult))
            vsep()
            V.emit(lambda v, xt=xt: v.tensor_scalar(
                xn_scr[:], xt, rstd, mrs, op0=OP.mult, op1=OP.add))
            vv = vhand()
            for c in range(NCH):
                b = BK[6 + (c % 2)]
                T.wait(s_v, vv)
                T.wait(s_g, g_consts)
                b.pre_write(T)
                tv = T.emit(lambda t, b=b, c=c: t.transpose(
                    b.ap[:, 0:128], xn_scr[:, c * 128:(c + 1) * 128], ident[:]),
                    inc=s_t)
                b.set_w(tv)
                tv_last = tv
                b.pre_read(V, s_t)
                rv = V.emit(lambda v, b=b, c=c, ii=ii, cs=cs: v.tensor_copy(
                    xn_fm[:, c * cs + ii * 128:c * cs + (ii + 1) * 128],
                    b.ap[:, 0:128]), inc=s_v)
                b.set_r(V, rv)
        return vhand()

    def load_slot(slot, src, lo, n, wait_t, piece_cols=128):
        Y.wait(s_t, wait_t)
        return Y.emit(lambda y, slot=slot, src=src, lo=lo, n=n, c=piece_cols:
                      y.dma_start(
                          out=slot.rearrange("p (k c) -> p k c", c=c)[:, 0:n, :],
                          in_=src[lo:lo + n].rearrange("n p c -> p n c")),
                      inc=s_y, inc_k=16)

    def fm_proj(xn, xn_ready, slot, y_ready, nkc, tok_off, ntokens, tstride, evac):
        last = None
        nslices = (ntokens + 511) // 512
        for ns in range(nslices):
            nn = min(512, ntokens - ns * 512)
            b = BK[ns % 2]
            T.wait(s_y, y_ready)
            T.wait(s_v, xn_ready)
            b.pre_write(T)
            for kc in range(nkc):
                mv = T.emit(lambda t, b=b, kc=kc, ns=ns, nn=nn, slot=slot, xn=xn,
                            tok_off=tok_off, tstride=tstride, nkc=nkc: t.matmul(
                    b.ap[:, 0:nn],
                    slot[:, kc * 128:(kc + 1) * 128],
                    xn[:, kc * tstride + tok_off + ns * 512:
                       kc * tstride + tok_off + ns * 512 + nn],
                    start=(kc == 0), stop=(kc == nkc - 1)),
                    inc=(s_t if kc == nkc - 1 else None))
            b.set_w(mv)
            last = mv
            b.pre_read(V, s_t)
            rv = evac(b, ns, nn)
            b.set_r(V, rv)
        return last

    def emit_rope(prj, swp, rot, Crt, Srt, W, ready_v):
        G.wait(s_v, vhand())
        gv = None
        for h0 in range(2):
            p0 = h0 * 64
            G.emit(lambda g, p0=p0: g.dma_start(
                out=swp[p0:p0 + 32, 0:W], in_=prj[p0 + 32:p0 + 64, 0:W]),
                inc=s_g, inc_k=16)
            G.emit(lambda g, p0=p0: g.dma_start(
                out=swp[p0 + 32:p0 + 64, 0:W], in_=prj[p0:p0 + 32, 0:W]),
                inc=s_g, inc_k=16)
        gv = G.emit(lambda g: g.drain(), inc=s_g, inc_k=16)
        V.wait(s_g, gv)
        V.emit(lambda v: v.tensor_tensor(rot[:, 0:W], prj[:, 0:W], Crt[:, 0:W],
                                         op=OP.mult))
        V.emit(lambda v: v.tensor_tensor(swp[:, 0:W], swp[:, 0:W], Srt[:, 0:W],
                                         op=OP.mult))
        vsep()
        V.emit(lambda v: v.tensor_tensor(rot[:, 0:W], rot[:, 0:W],
                                         swp[:, 0:W], op=OP.add))
        return vhand()

    def bcast_row(bc, width, waits):
        out_v = None
        for sl_ in range(width // 512):
            b = BK[6 + (sl_ % 2)]
            for sm, val in waits:
                T.wait(sm, val)
            T.wait(s_v, v_consts)
            b.pre_write(T)
            mv = T.emit(lambda t, b=b, bc=bc, sl_=sl_: t.matmul(
                b.ap[:, 0:512], ones_row[:],
                bc[0:1, sl_ * 512:sl_ * 512 + 512], start=True, stop=True),
                inc=s_t)
            b.set_w(mv)
            b.pre_read(V, s_t)
            out_v = V.emit(lambda v, b=b, bc=bc, sl_=sl_: v.tensor_copy(
                bc[:, sl_ * 512:sl_ * 512 + 512], b.ap[:, 0:512]), inc=s_v)
            b.set_r(V, out_v)
        return out_v

    def emit_gathers():
        S.emit(lambda s: s.activation(sml[:, 15:16], sml[:, 15:16], AF.Exp))
        shand()
        G.wait(s_g, s_g.n)
        G.emit(lambda g: g.drain(), inc=s_g, inc_k=16)
        s_c.n += 1
        G.q.append(lambda g: g.collective_compute(
            "AllGather", OP.bypass, replica_groups=[[0, 1], [2, 3], [4, 5], [6, 7]],
            ins=[k_in[:]], outs=[k_out[:]]).then_inc(s_c.h, 1))
        s_c.n += 1
        G.q.append(lambda g: g.collective_compute(
            "AllGather", OP.bypass, replica_groups=[[0, 1], [2, 3], [4, 5], [6, 7]],
            ins=[v_in[:]], outs=[v_out[:]]).then_inc(s_c.h, 1))
        s_c.n += 1
        fl_val = s_c.n
        G.q.append(lambda g: g.collective_compute(
            "AllReduce", OP.add, replica_groups=[[0, 1], [2, 3], [4, 5], [6, 7]],
            ins=[fl_in[:]], outs=[fl_ar[:]]).then_inc(s_c.h, 1))
        return fl_val, fl_val

    def attn_core(causal, bias_main, kscale, qscale, out_w,
                  cc_k_val, cc_v_val, q_spilled):
        kbuf = [pool.alloc(4096), pool.alloc(4096)]
        vbuf = [pool.alloc(32 * 130), pool.alloc(32 * 130)]
        qbuf = [pool.alloc(2048), pool.alloc(2048)]
        pt = [pool.alloc(512), pool.alloc(512)]
        outp = pool.alloc(2048)
        outf = pool.alloc(2048)
        wo_sl = [pool.alloc(1024), pool.alloc(1024)]

        pair_done_t = [0, 0]
        wo_used_t = [0, 0]
        load_ready = [None, None]
        pt_guard = [0, 0]
        linv = sml[:, 8:9]

        if causal:
            groups = [(j, 4 * j + 4, j * 256, 256, 2) for j in range(8)]
        else:
            groups = [(g, 32, g * 512, 512, 4) for g in range(4)]

        def load_pair(p):
            par = p % 2
            G.wait(s_t, pair_done_t[par])
            G.wait(s_c, cc_k_val)
            kv = kbuf[par].rearrange("p (j w s) -> p j w s", w=2, s=256)
            for r in range(2):
                G.emit(lambda g, kv=kv, r=r, p=p: g.dma_start(
                    out=kv[:, :, r, :],
                    in_=k_out[r, p * 128:(p + 1) * 128, :]
                        .rearrange("p (j s) -> p j s", s=256)),
                    inc=s_g, inc_k=16)
            G.wait(s_c, cc_v_val)
            vb_v = vbuf[par].rearrange("p (j r w c) -> p j r w c", r=2, w=2, c=130)
            for r in range(2):
                for w in range(2):
                    G.emit(lambda g, vb_v=vb_v, r=r, w=w, p=p: g.dma_start(
                        out=vb_v[:, :, r, w, :],
                        in_=v_out[r].rearrange(
                            "(j w) p r2 c -> p w j r2 c", w=2)[:, w, :, p, :]),
                        inc=s_g, inc_k=16)
            G.emit(lambda g, p=p, par=par: g.dma_start(
                out=qbuf[par], in_=q_sp[p]), inc=s_g, inc_k=16)
            yv = G.emit(lambda g: g.drain(), inc=s_g, inc_k=16)
            vvv = None
            if kscale is not None:
                (nwk_, rbk_), (nwq_, rbq_) = kscale
                V.wait(s_g, yv)
                V.wait(s_t, pair_done_t[par])
                V.emit(lambda v, par=par, p=p: v.scalar_tensor_tensor(
                    kbuf[par], kbuf[par], nwk_[:, p:p + 1], rbk_,
                    op0=OP.mult, op1=OP.mult))
                V.emit(lambda v, par=par, p=p: v.scalar_tensor_tensor(
                    qbuf[par], qbuf[par], nwq_[:, p:p + 1], rbq_,
                    op0=OP.mult, op1=OP.mult))
                vvv = vhand()
            load_ready[par] = (yv, vvv)
            Y.wait(s_t, wo_used_t[par])
            return Y.emit(lambda y, p=p, par=par: y.dma_start(
                out=wo_sl[par], in_=out_w[p]), inc=s_y, inc_k=16)

        if PARANOID & 16:
            T.wait(s_c, cc_v_val)
            BK[7].pre_write(T)
            for i_ in range(1000):
                dm = T.emit(lambda t: t.matmul(
                    BK[7].ap[:, 0:512], ident[:], pool_t[:, 0:512],
                    start=True, stop=True), inc=(s_t if i_ == 999 else None))
            BK[7].set_w(dm)
            Y.wait(s_t, dm)
        ywo_next = load_pair(0)
        for p in range(NPAIR):
            par = p % 2
            yv, vscale = load_ready[par]
            ywo_cur = ywo_next
            if p + 1 < NPAIR:
                ywo_next = load_pair(p + 1)
            kb, vb, qb = kbuf[par], vbuf[par], qbuf[par]
            groups_eff = ([groups[0]] + groups) if p == 0 else groups
            for (gi, bound, qoff, qcols, navq) in groups_eff:
                for h in range(2):
                    avb = BK[4 + h]
                    avb.pre_write(T)
                    pend = None
                    for kc in range(bound):
                        sbk = BK[kc % 2]
                        T.wait(s_g, yv)
                        if vscale is not None:
                            T.wait(s_v, vscale)
                        sbk.pre_write(T)
                        mv = T.emit(lambda t, sbk=sbk, kb=kb, qb=qb, h=h, kc=kc,
                                    qoff=qoff, qcols=qcols: t.matmul(
                            sbk.ap[:, 0:qcols],
                            kb[h * 64:(h + 1) * 64, kc * 128:(kc + 1) * 128],
                            qb[h * 64:(h + 1) * 64, qoff:qoff + qcols],
                            start=True, stop=True), inc=s_t)
                        sbk.set_w(mv)
                        S.wait(s_t, mv)
                        S.wait(s_t, pt_guard[kc % 2])
                        btile = (pbias_sb if (causal and kc >= bound - 2)
                                 else bias_main)
                        S.wait(s_v, v_consts)
                        S.wait(s_g, g_consts)
                        sbk.pre_read(S, s_t)
                        ev = S.emit(lambda s, sbk=sbk, kc=kc, qcols=qcols,
                                    btile=btile: s.activation(
                            pt[kc % 2][:, 0:qcols], sbk.ap[:, 0:qcols], AF.Exp,
                            bias=btile[:], scale=SCALE), inc=s_s)
                        sbk.set_r(S, ev)
                        if pend is not None:
                            pkc = pend
                            T.wait(s_s, ev)
                            for qs in range(navq):
                                av = T.emit(lambda t, avb=avb, pkc=pkc, qs=qs,
                                            h=h, vb=vb: t.matmul(
                                    avb.ap[:, qs * 65:qs * 65 + 65],
                                    pt[pkc % 2][:, qs * 128:qs * 128 + 128],
                                    vb[:, pkc * 130 + h * 65:
                                       pkc * 130 + h * 65 + 65],
                                    start=(pkc == 0 and qs == 0), stop=False),
                                    inc=(s_t if qs == navq - 1 else None))
                            pt_guard[pkc % 2] = av
                        pend = kc
                    pkc = pend
                    T.wait(s_s, shand())
                    for qs in range(navq):
                        av = T.emit(lambda t, avb=avb, pkc=pkc, qs=qs, h=h,
                                    vb=vb: t.matmul(
                            avb.ap[:, qs * 65:qs * 65 + 65],
                            pt[pkc % 2][:, qs * 128:qs * 128 + 128],
                            vb[:, pkc * 130 + h * 65:pkc * 130 + h * 65 + 65],
                            start=(pkc == 0 and qs == 0), stop=True),
                            inc=(s_t if qs == navq - 1 else None))
                    pt_guard[pkc % 2] = av
                    avb.set_w(av)
                    avb.pre_read(V, s_t)
                    if PARANOID & 4:
                        V.wait(s_t, s_t.n)
                        V.wait(s_s, s_s.n)
                    rv = None
                    for qs in range(navq):
                        V.emit(lambda v, avb=avb, qs=qs: v.reciprocal(
                            sml[:, 8 + qs:9 + qs],
                            avb.ap[:, qs * 65 + 64:qs * 65 + 65]))
                    if navq == 2:
                        vsep()
                    for qs in range(navq):
                        lt = gi * (2 if causal else 4) + qs
                        rv = V.emit(lambda v, avb=avb, qs=qs, lt=lt, h=h:
                                    v.tensor_scalar(
                            outp[:, lt * 128 + h * 64:lt * 128 + h * 64 + 64],
                            avb.ap[:, qs * 65:qs * 65 + 64],
                            sml[:, 8 + qs:9 + qs], None, op0=OP.mult), inc=s_v)
                    avb.set_r(V, rv)
                    if causal and p == 0 and gi == 0 and h == 0:
                        dbg_dump("pt0", pt[0], [128, 512])
                        dbg_dump("pt1", pt[1], [128, 512])
                        dbg_dump("outph0", outp[:, 0:256], [128, 256])
            if causal and p == 0:
                dbg_dump("outp0", outp, [128, 2048])
            out_ready = vhand()
            for lt in range(NTT):
                b = BK[6 + (lt % 2)]
                T.wait(s_v, out_ready)
                b.pre_write(T)
                tv = T.emit(lambda t, b=b, lt=lt: t.transpose(
                    b.ap[:, 0:128], outp[:, lt * 128:(lt + 1) * 128], ident[:]),
                    inc=s_t)
                b.set_w(tv)
                b.pre_read(V, s_t)
                rv = V.emit(lambda v, b=b, lt=lt: v.tensor_copy(
                    outf[:, lt * 128:(lt + 1) * 128], b.ap[:, 0:128]), inc=s_v)
                b.set_r(V, rv)
            outf_ready = vhand()
            mv = None
            for lt in range(NTT):
                for sl in range(2):
                    b = BK[2 + ((lt * 2 + sl) % 2)]
                    T.wait(s_v, outf_ready)
                    T.wait(s_y, ywo_cur)
                    b.pre_write(T)
                    mv = T.emit(lambda t, b=b, lt=lt, sl=sl, par=par: t.matmul(
                        b.ap[:, 0:512],
                        outf[:, lt * 128:(lt + 1) * 128],
                        wo_sl[par][:, sl * 512:sl * 512 + 512],
                        start=True, stop=True), inc=s_t)
                    b.set_w(mv)
                    b.pre_read(V, s_t)
                    rv = V.emit(lambda v, b=b, lt=lt, sl=sl: v.tensor_tensor(
                        x_tm[:, lt * 1024 + sl * 512:lt * 1024 + sl * 512 + 512],
                        b.ap[:, 0:512],
                        x_tm[:, lt * 1024 + sl * 512:lt * 1024 + sl * 512 + 512],
                        op=OP.add), inc=s_v)
                    b.set_r(V, rv)
            wo_used_t[par] = mv
            pair_done_t[par] = mv
        if causal:
            dbg_dump("kbuf7", kbuf[1], [128, 4096])
            dbg_dump("qbuf7", qbuf[1], [128, 2048])
            dbg_dump("vbuf7", vbuf[1], [128, 32 * 130])
            dbg_dump("outp7", outp, [128, 2048])

    def phase_attn_a():
        pool.push()
        xn_fm = pool.alloc(8 * 1024)
        xn_scr = pool.alloc(1024)
        wsl = [pool.alloc(1024), pool.alloc(1024)]
        prj = pool.alloc(1024)
        swp = pool.alloc(1024)
        rot = pool.alloc(1024)
        lat = pool.alloc(4 * 1024)
        vw = pool.alloc(4096)
        vstg = pool.alloc(8 * 130)
        Crt = pool.alloc(1024)
        Srt = pool.alloc(1024)

        slot_use = [0, 0]
        si = 0
        k_in_v = k_in[0:1024, :].rearrange("(c p) t -> c p t", p=128)
        g_last_spill = 0
        v_spill_val = 0
        vset = 0
        ywv = load_slot(vw, wvu, 0, 4, 0, piece_cols=1024)
        dbg_dump("x0", x_tm, [128, NTT * 1024])
        for half in range(2):
            ho = half * 1024
            V.wait(s_t, s_t.n)
            xn_ready = emit_ln(xn_fm, xn_scr, tile_lo=half * 8, ntiles=8)
            dbg_dump("xn1" if half == 0 else "xn1b", xn_fm, [128, 8 * 1024])
            gC = G.emit(lambda g, ho=ho: g.dma_start(
                out=Crt, in_=ropeC[:, ho:ho + 1024]), inc=s_g, inc_k=16)
            G.emit(lambda g, ho=ho: g.dma_start(
                out=Srt, in_=ropeS[:, ho:ho + 1024]), inc=s_g, inc_k=16)
            gS2 = G.emit(lambda g: g.drain(), inc=s_g, inc_k=16)
            V.wait(s_g, gS2)
            for mc in range(8):
                sl = wsl[si % 2]
                yv = load_slot(sl, wqa, mc * 8, 8, slot_use[si % 2])

                def evq(b, ns, nn):
                    V.wait(s_g, g_last_spill)
                    return V.emit(lambda v, b=b, ns=ns, nn=nn: v.tensor_copy(
                        prj[:, ns * 512:ns * 512 + nn], b.ap[:, 0:nn]), inc=s_v)
                lu = fm_proj(xn_fm, xn_ready, sl, yv, 8, 0, 1024, 1024, evq)
                slot_use[si % 2] = lu
                si += 1
                rv = emit_rope(prj, swp, rot, Crt, Srt, 1024, s_v.n)
                G.wait(s_v, rv)
                g_last_spill = G.emit(lambda g, mc=mc, ho=ho: g.dma_start(
                    out=q_sp[mc][:, ho:ho + 1024], in_=rot[:, 0:1024]),
                    inc=s_g, inc_k=16)
            for mc in range(4):
                sl = wsl[si % 2]
                yv = load_slot(sl, wdn, mc * 8, 8, slot_use[si % 2])

                def evl(b, ns, nn, mc=mc):
                    V.wait(s_t, s_t.n)
                    return V.emit(lambda v, b=b, ns=ns, nn=nn, mc=mc: v.tensor_copy(
                        lat[:, mc * 1024 + ns * 512:mc * 1024 + ns * 512 + nn],
                        b.ap[:, 0:nn]), inc=s_v)
                lu = fm_proj(xn_fm, xn_ready, sl, yv, 8, 0, 1024, 1024, evl)
                slot_use[si % 2] = lu
                si += 1
            lat_ready = vhand()
            for mc in range(8):
                sl = wsl[si % 2]
                yv = load_slot(sl, wku, mc * 4, 4, slot_use[si % 2])

                def evk(b, ns, nn):
                    V.wait(s_g, g_last_spill)
                    return V.emit(lambda v, b=b, ns=ns, nn=nn: v.tensor_copy(
                        prj[:, ns * 512:ns * 512 + nn], b.ap[:, 0:nn]), inc=s_v)
                lu = fm_proj(lat, lat_ready, sl, yv, 4, 0, 1024, 1024, evk)
                slot_use[si % 2] = lu
                si += 1
                rv = emit_rope(prj, swp, rot, Crt, Srt, 1024, s_v.n)
                G.wait(s_v, rv)
                g_last_spill = G.emit(lambda g, mc=mc, ho=ho: g.dma_start(
                    out=k_in_v[mc][:, ho:ho + 1024], in_=rot[:, 0:1024]),
                    inc=s_g, inc_k=16)
            if half == 0:
                V.emit(lambda v: v.memset(
                    vstg.rearrange("p (r c) -> p r c", c=130)[:, :, 64:65], 1.0))
                vset = V.emit(lambda v: v.memset(
                    vstg.rearrange("p (r c) -> p r c", c=130)[:, :, 129:130], 1.0),
                    inc=s_v)
            for tt in range(8):
                lt = half * 8 + tt
                rv = None
                for sl_ in range(2):
                    b = BK[2 + sl_]
                    T.wait(s_y, ywv)
                    T.wait(s_v, lat_ready)
                    b.pre_write(T)
                    for kc in range(4):
                        mv = T.emit(lambda t, b=b, kc=kc, sl_=sl_, tt=tt: t.matmul(
                            b.ap[:, 0:512],
                            lat[:, kc * 1024 + tt * 128:kc * 1024 + (tt + 1) * 128],
                            vw[:, kc * 1024 + sl_ * 512:kc * 1024 + sl_ * 512 + 512],
                            start=(kc == 0), stop=(kc == 3)),
                            inc=(s_t if kc == 3 else None))
                    b.set_w(mv)
                    b.pre_read(V, s_t)
                    V.wait(s_v, vset)
                    V.wait(s_g, v_spill_val)
                    rv = V.emit(lambda v, b=b, sl_=sl_: v.tensor_copy(
                        vstg.rearrange("p (r h c) -> p r h c", h=2, c=65)
                            [:, sl_ * 4:(sl_ + 1) * 4, :, 0:64],
                        b.ap[:, 0:512]), inc=s_v)
                    b.set_r(V, rv)
                G.wait(s_v, vhand())
                v_spill_val = G.emit(lambda g, lt=lt: g.dma_start(
                    out=v_in[lt],
                    in_=vstg.rearrange("p (r c) -> p r c", c=130)),
                    inc=s_g, inc_k=16)
        q_spill_val = s_g.n
        cc_k_val, cc_v_val = emit_gathers()
        dbg_dump("qsp1", q_sp[:], [8, 128, NTOK])
        dbg_dump("kin1", k_in[:], [1025, NTOK])
        dbg_dump("vin1", v_in[:], [NTT, 128, 8, 130])
        pool.pop()
        barrier_all()
        pool.push()
        attn_core(True, cba, None, None, woa, cc_k_val, cc_v_val, q_spill_val)
        pool.pop()
        barrier_all()
        dbg_dump("x1", x_tm, [128, NTT * 1024])
        flush()

    def phase_ffn(tag):
        pool.push()
        xn_fm = pool.alloc(8 * 512)
        xn_scr = pool.alloc(1024)
        wslg_ = [pool.alloc(8192), pool.alloc(8192)]
        wslu = pool.alloc(8192)
        hb = pool.alloc(4096)
        rr = pool.alloc(8)
        e8 = pool.alloc(8)
        z8 = pool.alloc(8)
        mk8 = pool.alloc(8)

        nm1 = sml[:, 16:17]; ssum = sml[:, 17:18]; sinv = sml[:, 18:19]
        m1p = sml[:, 19:20]; m2p = sml[:, 20:21]

        gate_use_ = [0, 0]
        up_use = 0
        prev_down = 0
        ei_ = 0
        for qt in range(4):
            V.wait(s_t, s_t.n)
            xn_ready = emit_ln(xn_fm, xn_scr, tile_lo=qt * 4, ntiles=4)
            for i4 in range(4):
                i = qt * 4 + i4
                b = BK[2 + (i4 % 2)]
                T.wait(s_v, xn_ready)
                T.wait(s_g, g_consts)
                b.pre_write(T)
                for kc in range(8):
                    mv = T.emit(lambda t, b=b, kc=kc, i4=i4: t.matmul(
                        b.ap[:, 0:8],
                        xn_fm[:, kc * 512 + i4 * 128:kc * 512 + (i4 + 1) * 128],
                        rw_sb[:, kc * 8:(kc + 1) * 8],
                        start=(kc == 0), stop=(kc == 7)),
                        inc=(s_t if kc == 7 else None))
                b.set_w(mv)
                b.pre_read(V, s_t)
                V.emit(lambda v, b=b: v.tensor_copy(rr, b.ap[:, 0:8]))
                vsep()
                V.emit(lambda v: v.reduce_max(nm1, rr, axis=AX.X, negate=True))
                vv = vhand()
                b.set_r(V, vv)
                S.wait(s_v, vv)
                S.emit(lambda s: s.activation(e8, rr, AF.Exp, bias=nm1,
                                              scale=1.0))
                sv = shand()
                V.wait(s_s, sv)
                V.emit(lambda v: v.reduce_sum(ssum, e8, axis=AX.X))
                vsep()
                V.emit(lambda v: v.reciprocal(sinv, ssum))
                vsep()
                V.emit(lambda v: v.tensor_scalar(e8, e8, sinv, None, op0=OP.mult))
                vsep()
                V.emit(lambda v: v.reduce_max(m1p, e8, axis=AX.X))
                vsep()
                V.emit(lambda v: v.tensor_scalar(mk8, e8, m1p, None, op0=OP.is_ge))
                vsep()
                V.emit(lambda v: v.scalar_tensor_tensor(z8, mk8, -2.0, e8,
                                                        op0=OP.mult, op1=OP.add))
                vsep()
                V.emit(lambda v: v.reduce_max(m2p, z8, axis=AX.X))
                vsep()
                V.emit(lambda v: v.tensor_scalar(mk8, e8, m2p, None, op0=OP.is_ge))
                vsep()
                V.emit(lambda v, i=i: v.tensor_tensor(
                    c_tm[:, i * 8:(i + 1) * 8], e8, mk8, op=OP.mult), inc=s_v)
            c_ready = s_v.n
            for e in range(10):
                wslg = wslg_[ei_ % 2]
                y_g = load_slot(wslg, wg[e], 0, 64, gate_use_[ei_ % 2])
                y_u = load_slot(wslu, wu[e], 0, 64, up_use)
                silu_last = 0
                for mc in range(8):
                    b = BK[mc % 2]
                    T.wait(s_y, y_g)
                    T.wait(s_v, xn_ready)
                    b.pre_write(T)
                    for kc in range(8):
                        mv = T.emit(lambda t, b=b, mc=mc, kc=kc, wslg=wslg:
                                    t.matmul(
                            b.ap[:, 0:512],
                            wslg[:, (mc * 8 + kc) * 128:(mc * 8 + kc + 1) * 128],
                            xn_fm[:, kc * 512:kc * 512 + 512],
                            start=(kc == 0), stop=(kc == 7)),
                            inc=(s_t if kc == 7 else None))
                    b.set_w(mv)
                    if mc == 7:
                        gate_use_[ei_ % 2] = mv
                    b.pre_read(S, s_t)
                    S.wait(s_t, prev_down)
                    S.emit(lambda s, b=b, mc=mc: s.activation(
                        hb[:, mc * 512:(mc + 1) * 512], b.ap[:, 0:512], AF.Silu))
                    sv = shand()
                    b.set_r(S, sv)
                    silu_last = sv
                y_d = load_slot(wslg, wd[e], 0, 8, gate_use_[ei_ % 2],
                                piece_cols=1024)
                for mc in range(8):
                    b = BK[2 + (mc % 2)]
                    T.wait(s_y, y_u)
                    b.pre_write(T)
                    for kc in range(8):
                        mv = T.emit(lambda t, b=b, mc=mc, kc=kc: t.matmul(
                            b.ap[:, 0:512],
                            wslu[:, (mc * 8 + kc) * 128:(mc * 8 + kc + 1) * 128],
                            xn_fm[:, kc * 512:kc * 512 + 512],
                            start=(kc == 0), stop=(kc == 7)),
                            inc=(s_t if kc == 7 else None))
                    b.set_w(mv)
                    if mc == 7:
                        up_use = mv
                    b.pre_read(V, s_t)
                    V.wait(s_s, silu_last)
                    rv = V.emit(lambda v, b=b, mc=mc: v.scalar_tensor_tensor(
                        hb[:, mc * 512:(mc + 1) * 512], b.ap[:, 0:512], 1.0,
                        hb[:, mc * 512:(mc + 1) * 512],
                        op0=OP.mult, op1=OP.mult), inc=s_v)
                    b.set_r(V, rv)
                h_ready = vhand()
                for tm in range(4):
                    gt = qt * 4 + tm
                    for sl_ in range(2):
                        b = BK[4 + sl_]
                        T.wait(s_y, y_d)
                        T.wait(s_v, h_ready)
                        b.pre_write(T)
                        for fc in range(8):
                            mv = T.emit(lambda t, b=b, fc=fc, tm=tm, sl_=sl_,
                                        wslg=wslg: t.matmul(
                                b.ap[:, 0:512],
                                hb[:, fc * 512 + tm * 128:fc * 512 + (tm + 1) * 128],
                                wslg[:, fc * 1024 + sl_ * 512:
                                     fc * 1024 + sl_ * 512 + 512],
                                start=(fc == 0), stop=(fc == 7)),
                                inc=(s_t if fc == 7 else None))
                        b.set_w(mv)
                        b.pre_read(V, s_t)
                        V.wait(s_v, c_ready)
                        if e < 8:
                            rv = V.emit(lambda v, b=b, gt=gt, sl_=sl_, e=e:
                                        v.scalar_tensor_tensor(
                                x_tm[:, gt * 1024 + sl_ * 512:
                                     gt * 1024 + sl_ * 512 + 512],
                                b.ap[:, 0:512],
                                c_tm[:, gt * 8 + e:gt * 8 + e + 1],
                                x_tm[:, gt * 1024 + sl_ * 512:
                                     gt * 1024 + sl_ * 512 + 512],
                                op0=OP.mult, op1=OP.add), inc=s_v)
                        else:
                            rv = V.emit(lambda v, b=b, gt=gt, sl_=sl_:
                                        v.tensor_tensor(
                                x_tm[:, gt * 1024 + sl_ * 512:
                                     gt * 1024 + sl_ * 512 + 512],
                                b.ap[:, 0:512],
                                x_tm[:, gt * 1024 + sl_ * 512:
                                     gt * 1024 + sl_ * 512 + 512],
                                op=OP.add), inc=s_v)
                        b.set_r(V, rv)
                prev_down = s_t.n
                gate_use_[ei_ % 2] = max(gate_use_[ei_ % 2], prev_down)
                ei_ += 1
        pool.pop()
        barrier_all()
        dbg_dump("x_" + tag, x_tm, [128, NTT * 1024])
        flush()

    def phase_attn_b():
        pool.push()
        xn_fm = pool.alloc(8 * 1024)
        xn_scr = pool.alloc(1024)
        wsl = [pool.alloc(1024), pool.alloc(1024)]
        prj = pool.alloc(1024)
        sqs = pool.alloc(1024)
        stat = pool.alloc(2048)
        statq = stat[0:1, :]
        statk = stat[32:33, :]
        bvr_t = pool.alloc(1024)
        bv_row = bvr_t[0:1, :]
        vw = pool.alloc(8192)
        vstg = pool.alloc(8 * 130)
        bvb_bc = pool.alloc(1024)

        G.emit(lambda g: g.dma_start(out=bv_row, in_=bvb[:]), inc=s_g, inc_k=16)
        gb = G.emit(lambda g: g.drain(), inc=s_g, inc_k=16)
        for sl_ in range(2):
            b = BK[6 + sl_]
            T.wait(s_g, gb)
            T.wait(s_v, v_consts)
            b.pre_write(T)
            mv = T.emit(lambda t, b=b, sl_=sl_: t.matmul(
                b.ap[:, 0:512], ones_row[:],
                bv_row[:, sl_ * 512:sl_ * 512 + 512], start=True, stop=True),
                inc=s_t)
            b.set_w(mv)
            b.pre_read(V, s_t)
            rv = V.emit(lambda v, b=b, sl_=sl_: v.tensor_copy(
                bvb_bc[:, sl_ * 512:sl_ * 512 + 512], b.ap[:, 0:512]), inc=s_v)
            b.set_r(V, rv)
        bias_bc_ready = s_v.n

        V.emit(lambda v: v.memset(statq, 0.0))
        vz = V.emit(lambda v: v.memset(statk, 0.0), inc=s_v)
        V.emit(lambda v: v.memset(
            vstg.rearrange("p (r c) -> p r c", c=130)[:, :, 64:65], 1.0))
        vset = V.emit(lambda v: v.memset(
            vstg.rearrange("p (r c) -> p r c", c=130)[:, :, 129:130], 1.0),
            inc=s_v)
        ywv = load_slot(vw, wvb, 0, 8, 0, piece_cols=1024)
        v_spill_val = 0

        slot_use = [0, 0]
        si = 0
        g_last_spill = 0
        prev_sq = 0
        k_in_v = k_in[0:1024, :].rearrange("(c p) t -> c p t", p=128)

        def qk_chunks(xn_ready, ho, wsrc, bias_tile, stat_row, spill_fn):
            nonlocal si, g_last_spill, prev_sq
            for mc in range(8):
                sl = wsl[si % 2]
                yv = load_slot(sl, wsrc, mc * 8, 8, slot_use[si % 2])

                def ev(b, ns, nn, mc=mc, bias_tile=bias_tile):
                    V.wait(s_g, g_last_spill)
                    V.wait(s_s, prev_sq)
                    return V.emit(lambda v, b=b, ns=ns, nn=nn, mc=mc:
                                  v.tensor_scalar(
                        prj[:, ns * 512:ns * 512 + nn], b.ap[:, 0:nn],
                        bias_tile[:, mc:mc + 1], None, op0=OP.add), inc=s_v)
                lu = fm_proj(xn_fm, xn_ready, sl, yv, 8, 0, 1024, 1024, ev)
                slot_use[si % 2] = lu
                si += 1
                prj_v = vhand()
                S.wait(s_v, prj_v)
                S.emit(lambda s: s.activation(sqs, prj, AF.Square))
                sv = shand()
                prev_sq = sv
                for ns in range(2):
                    b = BK[6 + (ns % 2)]
                    T.wait(s_s, sv)
                    b.pre_write(T)
                    mv = T.emit(lambda t, b=b, ns=ns: t.matmul(
                        b.ap[0:1, 0:512], ones_col[:],
                        sqs[:, ns * 512:ns * 512 + 512], start=True, stop=True),
                        inc=s_t)
                    b.set_w(mv)
                    b.pre_read(V, s_t)
                    V.wait(s_v, vz)
                    rv = V.emit(lambda v, b=b, ns=ns, ho=ho, stat_row=stat_row:
                                v.tensor_tensor(
                        stat_row[:, ho + ns * 512:ho + ns * 512 + 512],
                        b.ap[0:1, 0:512],
                        stat_row[:, ho + ns * 512:ho + ns * 512 + 512],
                        op=OP.add), inc=s_v)
                    b.set_r(V, rv)
                G.wait(s_v, vhand())
                g_last_spill = spill_fn(mc)
            return g_last_spill

        q_spill_val = 0
        for half in range(2):
            ho = half * 1024
            V.wait(s_t, s_t.n)
            xn_ready = emit_ln(xn_fm, xn_scr, tile_lo=half * 8, ntiles=8)
            q_spill_val = qk_chunks(
                xn_ready, ho, wqb, bq_sb, statq,
                lambda mc, ho=ho: G.emit(lambda g, mc=mc, ho=ho: g.dma_start(
                    out=q_sp[mc][:, ho:ho + 1024], in_=prj), inc=s_g, inc_k=16))
            qk_chunks(
                xn_ready, ho, wkb, bk_sb, statk,
                lambda mc, ho=ho: G.emit(lambda g, mc=mc, ho=ho: g.dma_start(
                    out=k_in_v[mc][:, ho:ho + 1024], in_=prj), inc=s_g, inc_k=16))
            for tt in range(8):
                lt = half * 8 + tt
                rv = None
                for sl_ in range(2):
                    b = BK[2 + sl_]
                    T.wait(s_y, ywv)
                    T.wait(s_v, xn_ready)
                    b.pre_write(T)
                    for kc in range(8):
                        mv = T.emit(lambda t, b=b, kc=kc, tt=tt, sl_=sl_: t.matmul(
                            b.ap[:, 0:512],
                            xn_fm[:, kc * 1024 + tt * 128:kc * 1024 + (tt + 1) * 128],
                            vw[:, kc * 1024 + sl_ * 512:kc * 1024 + sl_ * 512 + 512],
                            start=(kc == 0), stop=(kc == 7)),
                            inc=(s_t if kc == 7 else None))
                    b.set_w(mv)
                    b.pre_read(V, s_t)
                    V.wait(s_v, max(vset, bias_bc_ready))
                    V.wait(s_g, v_spill_val)
                    rv = V.emit(lambda v, b=b, sl_=sl_: v.tensor_tensor(
                        vstg.rearrange("p (r h c) -> p r h c", h=2, c=65)
                            [:, sl_ * 4:(sl_ + 1) * 4, :, 0:64],
                        b.ap[:, 0:512],
                        bvb_bc[:, sl_ * 512:sl_ * 512 + 512], op=OP.add), inc=s_v)
                    b.set_r(V, rv)
                G.wait(s_v, vhand())
                v_spill_val = G.emit(lambda g, lt=lt: g.dma_start(
                    out=v_in[lt],
                    in_=vstg.rearrange("p (r c) -> p r c", c=130)),
                    inc=s_g, inc_k=16)

        def fin_stat(row):
            V.emit(lambda v, row=row: v.tensor_scalar(
                row, row, 1.0 / D, RMS_EPS, op0=OP.mult, op1=OP.add))
            vsep()
            V.emit(lambda v, row=row: v.reciprocal(row, row))
            vv = vhand()
            S.wait(s_v, vv)
            S.emit(lambda s, row=row: s.activation(row, row, AF.Sqrt))
            return shand()
        sq_v = fin_stat(statq)
        sk_v = fin_stat(statk)
        G.wait(s_s, sk_v)
        G.emit(lambda g: g.dma_start(out=k_in[1024:1025, :], in_=statk),
               inc=s_g, inc_k=16)

        cc_k_val, cc_v_val = emit_gathers()
        dbg_dump("qsp3", q_sp[:], [8, 128, NTOK])
        dbg_dump("kin3", k_in[:], [1025, NTOK])

        pool.pop()
        barrier_all()
        pool.push()
        rq_bc = pool.alloc(2048)
        rk_bc = pool.alloc(4096)
        V.wait(s_s, sq_v)
        V.emit(lambda v: v.tensor_copy(rq_bc[0:1, :], statq))
        vv_q = vhand()
        G.wait(s_c, cc_k_val)
        for r in range(2):
            G.emit(lambda g, r=r: g.dma_start(
                out=rk_bc[0:1, :].rearrange("p (j w s) -> p j w s", w=2, s=256)
                    [:, :, r, :],
                in_=k_out[r, 1024:1025, :].rearrange("p (j s) -> p j s", s=256)),
                inc=s_g, inc_k=16)
        gv_k = G.emit(lambda g: g.drain(), inc=s_g, inc_k=16)
        bcast_row(rq_bc, 2048, [(s_v, vv_q)])
        bcast_row(rk_bc, 4096, [(s_g, gv_k)])
        Y.wait(s_v, s_v.n)
        G.wait(s_v, s_v.n)
        attn_core(False, cbb, ((nwk_sb, rk_bc), (nwq_sb, rq_bc)), None, wob,
                  cc_k_val, cc_v_val, q_spill_val)
        bo_row2 = pool.alloc(1024)
        bo_bc2 = pool.alloc(1024)
        G.emit(lambda g: g.dma_start(out=bo_row2[0:1, :], in_=bob[:]),
               inc=s_g, inc_k=16)
        gb2 = G.emit(lambda g: g.drain(), inc=s_g, inc_k=16)
        for sl_ in range(2):
            b = BK[6 + sl_]
            T.wait(s_g, gb2)
            b.pre_write(T)
            mv = T.emit(lambda t, b=b, sl_=sl_: t.matmul(
                b.ap[:, 0:512], ones_row[:],
                bo_row2[0:1, sl_ * 512:sl_ * 512 + 512], start=True, stop=True),
                inc=s_t)
            b.set_w(mv)
            b.pre_read(V, s_t)
            rv = V.emit(lambda v, b=b, sl_=sl_: v.tensor_copy(
                bo_bc2[:, sl_ * 512:sl_ * 512 + 512], b.ap[:, 0:512]), inc=s_v)
            b.set_r(V, rv)
        for lt in range(NTT):
            V.emit(lambda v, lt=lt: v.tensor_tensor(
                x_tm[:, lt * 1024:(lt + 1) * 1024], bo_bc2,
                x_tm[:, lt * 1024:(lt + 1) * 1024], op=OP.add), inc=s_v)
        pool.pop()
        barrier_all()
        dbg_dump("x3", x_tm, [128, NTT * 1024])
        flush()

    flush()
    phase_attn_a()
    phase_ffn("f1")
    phase_attn_b()
    phase_ffn("f2")

    vhand()
    G.wait(s_v, s_v.n)
    y_v = x_tm.rearrange("p (t c) -> p t c", c=1024)
    gfinal = G.emit(lambda g: g.dma_start(
        out=y_out.rearrange("t p c -> p t c"), in_=y_v), inc=s_g, inc_k=16)
    G.q.append(lambda g, v=gfinal: g.wait_ge(s_g.h, v))
    flush()

    for cm in reversed(ctxs):
        cm.__exit__(None, None, None)
    return nc, dbg_requests


def _sl2(w):
    K, M = w.shape
    return np.ascontiguousarray(
        w.reshape(K // 128, 128, M // 128, 128).transpose(2, 0, 1, 3)
    ).reshape(-1, 128, 128)


def _rch(w):
    K, N = w.shape
    return np.ascontiguousarray(w.reshape(K // 128, 128, N))


_ROPE_PERM = np.concatenate([np.arange(0, 64, 2), np.arange(1, 64, 2)])


def _permute_cols(w):
    M = w.shape[1]
    return np.ascontiguousarray(
        w.reshape(w.shape[0], M // 64, 64)[:, :, _ROPE_PERM].reshape(w.shape))


def _rope_tables(pos):
    inv = 1.0 / 10000.0 ** (np.arange(0, 64, 2, dtype=np.float32) / 64)
    ang = np.outer(pos.astype(np.float32), inv)
    cos = np.asarray(np.cos(ang), np.float32)
    sin = np.asarray(np.sin(ang), np.float32)
    rows = np.arange(128)
    i = rows % 32
    sign = np.where((rows // 32) % 2 == 0, -1.0, 1.0).astype(np.float32)
    C = cos.T[i]
    S = sin.T[i] * sign[:, None]
    return np.ascontiguousarray(C), np.ascontiguousarray(S)


def _stage(I):
    w = {}
    w["wqa"] = _sl2(_permute_cols(np.asarray(I["L0_attn_a_wq"], np.float32)))
    w["wdn"] = _sl2(np.asarray(I["L0_attn_a_w_down"], np.float32))
    w["wku"] = _sl2(_permute_cols(np.asarray(I["L0_attn_a_wk_up"], np.float32)))
    w["wvu"] = _rch(np.asarray(I["L0_attn_a_wv_up"], np.float32))
    w["woa"] = _rch(np.asarray(I["L0_attn_a_wo"], np.float32))
    w["wqb"] = _sl2(np.ascontiguousarray(np.asarray(I["L0_attn_b_Wq"], np.float32).T))
    w["wkb"] = _sl2(np.ascontiguousarray(np.asarray(I["L0_attn_b_Wk"], np.float32).T))
    w["wvb"] = _rch(np.ascontiguousarray(np.asarray(I["L0_attn_b_Wv"], np.float32).T))
    w["wob"] = _rch(np.ascontiguousarray(np.asarray(I["L0_attn_b_Wo"], np.float32).T))
    w["bqb"] = np.ascontiguousarray(np.asarray(I["L0_attn_b_bq"], np.float32).reshape(8, 128).T)
    w["bkb"] = np.ascontiguousarray(np.asarray(I["L0_attn_b_bk"], np.float32).reshape(8, 128).T)
    w["bvb"] = np.asarray(I["L0_attn_b_bv"], np.float32).reshape(1, 1024)
    w["bob"] = np.asarray(I["L0_attn_b_bo"], np.float32).reshape(1, 1024)
    w["nwq"] = np.ascontiguousarray(np.asarray(I["L0_attn_b_norm_q_w"], np.float32).reshape(8, 128).T)
    w["nwk"] = np.ascontiguousarray(np.asarray(I["L0_attn_b_norm_k_w"], np.float32).reshape(8, 128).T)
    gate = np.asarray(I["L0_ffn_gate_w"], np.float32)
    up = np.asarray(I["L0_ffn_up_w"], np.float32)
    down = np.asarray(I["L0_ffn_down_w"], np.float32)
    sg = np.asarray(I["L0_ffn_shared_gate_w"], np.float32)
    su = np.asarray(I["L0_ffn_shared_up_w"], np.float32)
    sd = np.asarray(I["L0_ffn_shared_down_w"], np.float32)
    w["wg"] = np.stack([_sl2(gate[e]) for e in range(8)]
                       + [_sl2(sg[:, :1024]), _sl2(sg[:, 1024:])])
    w["wu"] = np.stack([_sl2(up[e]) for e in range(8)]
                       + [_sl2(su[:, :1024]), _sl2(su[:, 1024:])])
    w["wd"] = np.stack([_rch(down[e]) for e in range(8)]
                       + [_rch(sd[:1024]), _rch(sd[1024:])])
    w["wr"] = _rch(np.asarray(I["L0_ffn_router_w"], np.float32))
    w["identw"] = np.eye(128, dtype=np.float32)
    return w


def _own_ids(m):
    pages = np.arange(m, 16, 2)
    return (pages[:, None] * 256 + np.arange(256)[None, :]).reshape(-1)


def kernel(**inputs):
    x = np.asarray(inputs["L0_attn_a_x"], np.float32)
    B, T, E = x.shape
    shared = _stage(inputs)
    nc, _ = build_program()
    in_maps = []
    ids_by_m = [_own_ids(0), _own_ids(1)]
    for core in range(8):
        b, m = core // 2, core % 2
        ids = ids_by_m[m]
        xc = np.ascontiguousarray(x[b, ids, :]).reshape(NTT, 128, D)
        C, S = _rope_tables(ids)
        im = dict(shared)
        im["x"] = xc
        im["ropeC"] = C
        im["ropeS"] = S
        im["pbias"] = np.full((128, 1), PAD_B if m == 0 else CB_A, np.float32)
        in_maps.append(im)
    res = run_bass_kernel_spmd(nc, in_maps, core_ids=list(range(8)))
    out = np.empty((B, T, E), np.float32)
    for core in range(8):
        b, m = core // 2, core % 2
        out[b, ids_by_m[m], :] = res.results[core]["y"].reshape(NTOK, D)
    return out
